# version 01
# baseline (speedup 1.0000x reference)
import numpy as np
import ml_dtypes
from contextlib import ExitStack

import concourse.bass as bass
import concourse.mybir as mybir
from concourse.bass_utils import run_bass_kernel_spmd

F32 = mybir.dt.float32
BF16 = mybir.dt.bfloat16
U8 = mybir.dt.uint8
AF = mybir.ActivationFunctionType
ALU = mybir.AluOpType
BF = ml_dtypes.bfloat16

D = 1024
T = 2048
S = 4096
NKT = D // 128
NTT = T // 512
EPS = 1e-5

PI = np.concatenate([np.arange(0, D, 2), np.arange(1, D, 2)])
PINV = np.argsort(PI)
PI2 = np.concatenate([np.arange(0, 2048, 4), np.arange(2, 2048, 4),
                      np.arange(1, 2048, 4), np.arange(3, 2048, 4)])


class Sem:
    def __init__(self, es, nc, name):
        self.h = es.enter_context(nc.semaphore(name))
        self.n = 0

    def inc(self, k=1):
        self.n += k
        return self.n


class QSlots:

    def __init__(self, es, nc, name, nslots=2):
        self.sems = [Sem(es, nc, f"{name}{i}") for i in range(nslots)]
        self.n = nslots

    def sem(self, slot):
        return self.sems[slot % self.n]

    def issue(self, e, slot, dma_instrs):
        s = self.sem(slot)
        for ins in dma_instrs:
            ins.then_inc(s.h, 16)
            s.inc(16)
        return s.n

    def snap(self):
        return [s.n for s in self.sems]

    def ft(self, base, i, per=16):
        return base[i % self.n] + per*(i // self.n + 1)

    def wait_all(self, e):
        for s in self.sems:
            e.wait_ge(s.h, s.n)


class Ctx:
    def __init__(self, nc, es, consts=()):
        self.nc = nc
        self.es = es
        self._blk = 0
        for nm in ["ld", "ld2", "ldg", "st", "mm", "evs", "evv", "evg", "cc",
                   "pp", "pe", "pv", "pc"]:
            setattr(self, nm, Sem(es, nc, nm))
        self._consts = {}
        consts = list(consts)
        if consts:
            self.cbank = es.enter_context(nc.sbuf_tensor("cbank", [128, len(consts)], F32))
            for i, v in enumerate(consts):
                self._consts[float(v)] = i
            with self.block("consts") as b:
                @b.vector
                def _(e):
                    for i, v in enumerate(consts):
                        e.memset(self.cbank[:, i:i+1], float(v))

    def c(self, v, parts=128):
        return self.cbank[0:parts, self._consts[float(v)]:self._consts[float(v)]+1]

    def q(self, name, nslots=2):
        if not hasattr(self, "_qs"):
            self._qs = {}
        if name not in self._qs:
            self._qs[name] = QSlots(self.es, self.nc, name, nslots)
        assert self._qs[name].n == nslots
        return self._qs[name]

    def block(self, name):
        self._blk += 1
        return self.nc.Block(f"{name}_{self._blk}", no_gpsimd_drain=True)



def emit_dequant_int8(ctx, H, qname, sname, dname, n_tiles, width, half_split, tag):
    nc = ctx.nc
    es = ExitStack()
    with es:
        q_sb = [es.enter_context(nc.sbuf_tensor(f"{tag}_q{i}", [128, width], U8)) for i in range(2)]
        qf = [es.enter_context(nc.sbuf_tensor(f"{tag}_qf{i}", [128, width], F32)) for i in range(2)]
        wdec = [es.enter_context(nc.sbuf_tensor(f"{tag}_w{i}", [128, width], BF16)) for i in range(2)]
        sct = [es.enter_context(nc.sbuf_tensor(f"{tag}_sc{i}", [128, width], F32)) for i in range(2)]

        qQ, sQ, wQ = ctx.q("qSY"), ctx.q("qGA"), ctx.q("qGB")
        bq, bs, bw = qQ.snap(), sQ.snap(), wQ.snap()
        per = 32 if half_split else 16
        b_evs, b_evv = ctx.evs.n, ctx.evv.n
        with ctx.block(f"{tag}_main") as b:
            @b.sync
            def _(e):
                for i in range(n_tiles):
                    if i >= 2:
                        e.wait_ge(ctx.evs.h, b_evs + (i - 1))
                    qQ.issue(e, i, [e.dma_start(out=q_sb[i % 2][:, :], in_=H[qname][i*128:(i+1)*128, :])])

            @b.scalar
            def _(e):
                for i in range(n_tiles):
                    e.wait_ge(qQ.sem(i).h, qQ.ft(bq, i))
                    if i >= 2:
                        e.wait_ge(ctx.evv.h, b_evv + (i - 1))
                    e.activation(out=qf[i % 2][:, :], in_=q_sb[i % 2][:, :], func=AF.Identity,
                                 bias=ctx.c(-128.0)).then_inc(ctx.evs.h, 1)
                ctx.evs.inc(n_tiles)

            @b.gpsimd
            def _(e):
                for i in range(n_tiles):
                    if i >= 2:
                        e.wait_ge(ctx.evv.h, b_evv + (i - 1))
                    if half_split:
                        sQ.issue(e, i, [
                            e.dma_start(out=sct[i % 2][0:64, :],
                                        in_=H[sname][2*i:2*i+1, :].partition_broadcast(64)),
                            e.dma_start(out=sct[i % 2][64:128, :],
                                        in_=H[sname][2*i+1:2*i+2, :].partition_broadcast(64))])
                    else:
                        sQ.issue(e, i, [e.dma_start(out=sct[i % 2][:, :],
                                                    in_=H[sname][i:i+1, :].partition_broadcast(128))])
                    e.wait_ge(ctx.evv.h, b_evv + i + 1)
                    wQ.issue(e, i, [e.dma_start(out=H[dname][i, :, :], in_=wdec[i % 2][:, :])])
                wQ.wait_all(e)

            @b.vector
            def _(e):
                for i in range(n_tiles):
                    e.wait_ge(ctx.evs.h, b_evs + i + 1)
                    e.wait_ge(sQ.sem(i).h, sQ.ft(bs, i, per))
                    if i >= 2:
                        e.wait_ge(wQ.sem(i).h, wQ.ft(bw, i - 2))
                    e.tensor_tensor(out=wdec[i % 2][:, :], in0=qf[i % 2][:, :], in1=sct[i % 2][:, :],
                                    op=ALU.mult).then_inc(ctx.evv.h, 1)
                ctx.evv.inc(n_tiles)


def emit_ln_stats(ctx, xb, lnb, tag, ntok=T):
    nc = ctx.nc
    nch = ntok // 512
    es = ExitStack()
    with es:
        sqb = [es.enter_context(nc.sbuf_tensor(f"{tag}_sq{i}", [128, 512], F32)) for i in range(2)]
        onesq = es.enter_context(nc.sbuf_tensor(f"{tag}_one", [128, 128], F32))
        st0 = [es.enter_context(nc.psum_tensor(f"{tag}_s0{i}", [128, 512], F32)) for i in range(2)]
        st1 = [es.enter_context(nc.psum_tensor(f"{tag}_s1{i}", [128, 512], F32)) for i in range(2)]

        with ctx.block(f"{tag}_pre") as b:
            @b.vector
            def _(e):
                e.memset(onesq[:, :], 1.0)

        b_evs, b_mm, b_evv = ctx.evs.n, ctx.mm.n, ctx.evv.n
        with ctx.block(f"{tag}_main") as b:
            @b.scalar
            def _(e):
                for c in range(nch):
                    for kt in range(NKT):
                        i = c*NKT + kt
                        if i >= 2:
                            e.wait_ge(ctx.mm.h, b_mm + (i - 1))
                        e.activation(out=sqb[i % 2][:, :], in_=xb[:, kt*ntok+c*512: kt*ntok+c*512+512],
                                     func=AF.Square).then_inc(ctx.evs.h, 1)
                ctx.evs.inc(nch*NKT)

            @b.tensor
            def _(e):
                for c in range(nch):
                    if c >= 2:
                        e.wait_ge(ctx.evv.h, b_evv + (c - 1))
                    for kt in range(NKT):
                        i = c*NKT + kt
                        e.wait_ge(ctx.evs.h, b_evs + i + 1)
                        e.matmul(out=st0[c % 2][:, :], lhsT=onesq[:, :],
                                 rhs=xb[:, kt*ntok+c*512: kt*ntok+c*512+512],
                                 start=(kt == 0), stop=(kt == NKT-1))
                        e.matmul(out=st1[c % 2][:, :], lhsT=onesq[:, :],
                                 rhs=sqb[i % 2][:, :],
                                 start=(kt == 0), stop=(kt == NKT-1)).then_inc(ctx.mm.h, 1)
                ctx.mm.inc(nch*NKT)

            @b.vector
            def _(e):
                for c in range(nch):
                    e.wait_ge(ctx.mm.h, b_mm + (c+1)*NKT)
                    e.tensor_copy(out=lnb[:, c*512:(c+1)*512], in_=st0[c % 2][:, :])
                    e.tensor_copy(out=lnb[:, ntok+c*512: ntok+(c+1)*512], in_=st1[c % 2][:, :]).then_inc(ctx.evv.h, 1)
                ctx.evv.inc(nch)


def emit_ln_derive(ctx, lnb, bc, tag, ntok=T, with_r=False):
    sg = lambda buf, j: buf[:, j*ntok:(j+1)*ntok]
    b_evv, b_evs = ctx.evv.n, ctx.evs.n
    with ctx.block(f"{tag}_drv") as b:
        @b.vector
        def _(e):
            e.tensor_scalar(out=sg(lnb, 0), in0=sg(lnb, 0), scalar1=1.0/D, scalar2=None, op0=ALU.mult)
            e.tensor_scalar(out=sg(lnb, 1), in0=sg(lnb, 1), scalar1=1.0/D, scalar2=None, op0=ALU.mult)
            e.tensor_tensor(out=sg(bc, 0), in0=sg(lnb, 0), in1=sg(lnb, 0), op=ALU.mult)
            e.tensor_tensor(out=sg(lnb, 1), in0=sg(lnb, 1), in1=sg(bc, 0),
                            op=ALU.subtract).then_inc(ctx.evv.h, 1)
            ctx.evv.inc()
            e.wait_ge(ctx.evs.h, b_evs + 1)
            e.reciprocal(out=sg(bc, 0), in_=sg(bc, 0))
            e.tensor_tensor(out=sg(bc, 1), in0=sg(lnb, 0), in1=sg(bc, 0), op=ALU.mult)
            if with_r:
                e.tensor_scalar(out=sg(bc, 2), in0=sg(bc, 0), scalar1=1.0, scalar2=None, op0=ALU.add)

        @b.scalar
        def _(e):
            e.wait_ge(ctx.evv.h, b_evv + 1)
            e.activation(out=sg(bc, 0), in_=sg(lnb, 1), func=AF.Sqrt,
                         bias=ctx.c(EPS)).then_inc(ctx.evs.h, 1)
            ctx.evs.inc()



def emit_p2(ctx, H, upto="all"):
    nc = ctx.nc

    emit_dequant_int8(ctx, H, "fc1q", "s1rows", "w1dec", 8, 4096, True, "dqA")
    emit_dequant_int8(ctx, H, "fc2q", "fc2s", "w2dec", 32, 1024, False, "dqB")
    if upto == "dq":
        return

    es = ExitStack()
    with es:
        SB = lambda *a: es.enter_context(nc.sbuf_tensor(*a))
        PS = lambda *a: es.enter_context(nc.psum_tensor(*a))

        hbuf = SB("p2_h", [128, NKT*T], BF16)
        gamma = SB("p2_gam", [128, NKT], F32)
        beta = SB("p2_bet", [128, NKT], F32)
        b2h = SB("p2_b2h", [128, NKT], F32)
        fc1b = SB("p2_f1b", [128, 32], F32)
        rt = [SB(f"p2_rt{i}", [128, 512], F32) for i in range(2)]

        ies = ExitStack()
        with ies:
            ISB = lambda *a: ies.enter_context(nc.sbuf_tensor(*a))
            xb = ISB("p2_x", [128, NKT*T], F32)
            lnb = ISB("p2_lnb", [128, 2*T], F32)
            bc = ISB("p2_bc", [128, 4*T], F32)
            sg = lambda buf, j: buf[:, j*T:(j+1)*T]

            with ctx.block("p2_load") as b:
                @b.sync
                def _(e):
                    e.dma_start(out=xb[:, :].rearrange("p (k t) -> p k t", k=NKT),
                                in_=H["xin"].rearrange("(k p) t -> p k t", p=128)).then_inc(ctx.ld.h, 16)
                    ctx.ld.inc(16)
                    for nm, dst in [("gamma", gamma), ("beta", beta), ("fc2b", b2h), ("fc1b", fc1b)]:
                        e.dma_start(out=dst[:, :], in_=H[nm][:, :]).then_inc(ctx.ld.h, 16)
                        ctx.ld.inc(16)
                    e.wait_ge(ctx.ld.h, ctx.ld.n)

            with ctx.block("p2_half") as b:
                @b.vector
                def _(e):
                    e.tensor_scalar(out=b2h[:, :], in0=b2h[:, :], scalar1=0.5, scalar2=None, op0=ALU.mult)

            emit_ln_stats(ctx, xb, lnb, "p2ln")

            b_evv, b_evs = ctx.evv.n, ctx.evs.n
            with ctx.block("p2_rows") as b:
                @b.vector
                def _(e):
                    e.tensor_scalar(out=sg(lnb, 0), in0=sg(lnb, 0), scalar1=1.0/D, scalar2=None, op0=ALU.mult)
                    e.tensor_scalar(out=sg(lnb, 1), in0=sg(lnb, 1), scalar1=1.0/D, scalar2=None, op0=ALU.mult)
                    e.tensor_tensor(out=sg(bc, 0), in0=sg(lnb, 0), in1=sg(lnb, 0), op=ALU.mult)
                    e.tensor_tensor(out=sg(lnb, 1), in0=sg(lnb, 1), in1=sg(bc, 0),
                                    op=ALU.subtract).then_inc(ctx.evv.h, 1)
                    ctx.evv.inc()
                    e.wait_ge(ctx.evs.h, b_evs + 1)
                    e.reciprocal(out=sg(bc, 0), in_=sg(bc, 0))
                    e.tensor_tensor(out=sg(bc, 1), in0=sg(bc, 0), in1=sg(bc, 0), op=ALU.mult)
                    e.tensor_tensor(out=sg(bc, 1), in0=sg(bc, 1), in1=sg(lnb, 1),
                                    op=ALU.mult).then_inc(ctx.evv.h, 1)
                    ctx.evv.inc()
                    e.wait_ge(ctx.evs.h, b_evs + 2)
                    e.reciprocal(out=sg(bc, 1), in_=sg(bc, 1))
                    e.tensor_scalar(out=sg(bc, 2), in0=sg(bc, 0), scalar1=1.0, scalar2=None, op0=ALU.add)
                    e.tensor_tensor(out=sg(bc, 3), in0=sg(lnb, 0), in1=sg(bc, 0), op=ALU.mult)
                    e.tensor_tensor(out=sg(bc, 0), in0=sg(bc, 0), in1=sg(bc, 1), op=ALU.mult)
                    e.tensor_tensor(out=sg(bc, 1), in0=sg(lnb, 0), in1=sg(bc, 0), op=ALU.mult)

                @b.scalar
                def _(e):
                    e.wait_ge(ctx.evv.h, b_evv + 1)
                    e.activation(out=sg(bc, 0), in_=sg(lnb, 1), func=AF.Sqrt,
                                 bias=ctx.c(EPS)).then_inc(ctx.evs.h, 1)
                    ctx.evs.inc()
                    e.wait_ge(ctx.evv.h, b_evv + 2)
                    e.activation(out=sg(bc, 1), in_=sg(bc, 1), func=AF.Sqrt,
                                 bias=ctx.c(EPS)).then_inc(ctx.evs.h, 1)
                    ctx.evs.inc()

            rQ = ctx.q("qGB")
            br = rQ.snap()
            b_evv = ctx.evv.n
            with ctx.block("p2_apply") as b:
                @b.vector
                def _(e):
                    for kt in range(NKT):
                        for c in range(NTT):
                            i = kt*NTT + c
                            sl = slice(kt*T + c*512, kt*T + c*512 + 512)
                            bcs = lambda j: slice(j*T + c*512, j*T + c*512 + 512)
                            if i >= 2:
                                e.wait_ge(rQ.sem(i).h, rQ.ft(br, i - 2))
                            e.tensor_tensor(out=rt[i % 2][:, :], in0=xb[:, sl], in1=bc[:, bcs(2)], op=ALU.mult)
                            e.tensor_tensor(out=rt[i % 2][:, :], in0=rt[i % 2][:, :], in1=bc[:, bcs(3)],
                                            op=ALU.subtract).then_inc(ctx.evv.h, 1)
                            e.tensor_tensor(out=xb[:, sl], in0=xb[:, sl], in1=bc[:, bcs(0)], op=ALU.mult)
                            e.tensor_tensor(out=xb[:, sl], in0=xb[:, sl], in1=bc[:, bcs(1)], op=ALU.subtract)
                            e.tensor_scalar(out=hbuf[:, sl], in0=xb[:, sl],
                                            scalar1=gamma[:, kt:kt+1], scalar2=beta[:, kt:kt+1],
                                            op0=ALU.mult, op1=ALU.add)
                    ctx.evv.inc(NKT*NTT)

                @b.gpsimd
                def _(e):
                    for kt in range(NKT):
                        for c in range(NTT):
                            i = kt*NTT + c
                            e.wait_ge(ctx.evv.h, b_evv + i + 1)
                            rQ.issue(e, i, [e.dma_start(out=H["rdr"][kt*128:(kt+1)*128, c*512:(c+1)*512],
                                                        in_=rt[i % 2][:, :])])
                    rQ.wait_all(e)

            if upto == "ln":
                with ctx.block("p2_dbg") as b:
                    @b.gpsimd
                    def _(e):
                        for kt in range(NKT):
                            e.dma_start(out=H["hdbg"][kt*128:(kt+1)*128, :],
                                        in_=hbuf[:, kt*T:(kt+1)*T]).then_inc(ctx.st.h, 16)
                            ctx.st.inc(16)
                        e.dma_start(out=H["bcdbg"][:, :], in_=bc[:, :]).then_inc(ctx.st.h, 16)
                        ctx.st.inc(16)
                        e.dma_start(out=H["lnbdbg"][:, :], in_=lnb[:, :]).then_inc(ctx.st.h, 16)
                        ctx.st.inc(16)
                        e.wait_ge(ctx.st.h, ctx.st.n)
        if upto == "ln":
            return

        ych = SB("p2_y", [128, 32*512], BF16)
        w1t = [SB(f"p2_w1t{i}", [128, 8*128], BF16) for i in range(2)]
        w2t = [SB(f"p2_w2t{i}", [128, 32*128], BF16) for i in range(2)]
        rrt = [SB(f"p2_rr{i}", [128, 512], F32) for i in range(2)]
        zt = [SB(f"p2_z{i}", [128, 512], F32) for i in range(2)]
        ps1 = [PS(f"p2_ps1{i}", [128, 512], F32) for i in range(2)]
        ps2 = [PS(f"p2_ps2{i}", [128, 512], F32) for i in range(2)]

        w1Q, w2Q, rrQ, ztQ = ctx.q("qSY"), ctx.q("qSC"), ctx.q("qGA"), ctx.q("qGB")
        bw1, bw2, brr, bzt = w1Q.snap(), w2Q.snap(), rrQ.snap(), ztQ.snap()
        b_mm, b_evs, b_evv = ctx.mm.n, ctx.evs.n, ctx.evv.n
        NMT1, NMT2 = 32, 8
        NCH = 1 if upto == "fc1" else NTT
        do_fc2 = (upto != "fc1")
        per_chunk = NMT1 + NMT2

        with ctx.block("p2_mm") as b:
            @b.sync
            def _(e):
                for c in range(NCH):
                    for mt in range(NMT1):
                        i = c*NMT1 + mt
                        if i >= 2:
                            cc, mm_ = divmod(i - 2, NMT1)
                            e.wait_ge(ctx.mm.h, b_mm + cc*(per_chunk if upto != "fc1" else NMT1) + mm_ + 1)
                        w1Q.issue(e, i, [e.dma_start(
                            out=w1t[i % 2][:, :].rearrange("p (k f) -> p k f", k=8),
                            in_=H["w1dec"][:, :, mt*128:(mt+1)*128].rearrange("k p f -> p k f"))])

            @b.scalar
            def _(e):
                pc_ = per_chunk if do_fc2 else NMT1
                for c in range(NCH):
                    for mt in range(NMT1):
                        e.wait_ge(ctx.mm.h, b_mm + c*pc_ + mt + 1)
                        e.activation(out=ych[:, mt*512:(mt+1)*512], in_=ps1[mt % 2][:, :],
                                     func=AF.Silu, bias=fc1b[:, mt:mt+1], scale=1.0).then_inc(ctx.evs.h, 1)
                    ctx.evs.inc(NMT1)
                    if do_fc2:
                        for mt in range(NMT2):
                            i = c*NMT2 + mt
                            if i >= 2:
                                cc, mm_ = divmod(i - 2, NMT2)
                                e.wait_ge(ctx.mm.h, b_mm + cc*per_chunk + NMT1 + mm_ + 1)
                            w2Q.issue(e, i, [e.dma_start(
                                out=w2t[i % 2][:, :].rearrange("p (k f) -> p k f", k=32),
                                in_=H["w2dec"][:, :, mt*128:(mt+1)*128].rearrange("k p f -> p k f"))])

            @b.gpsimd
            def _(e):
                if not do_fc2:
                    return
                for c in range(NCH):
                    for mt in range(NMT2):
                        i = c*NMT2 + mt
                        if i >= 2:
                            e.wait_ge(ctx.evv.h, b_evv + (i - 1))
                        rrQ.issue(e, i, [e.dma_start(
                            out=rrt[i % 2][:, :],
                            in_=H["rdr"][mt*128:(mt+1)*128, c*512:(c+1)*512])])
                        e.wait_ge(ctx.evv.h, b_evv + i + 1)
                        ztQ.issue(e, i, [e.dma_start(
                            out=H["xout"][mt*128:(mt+1)*128, c*512:(c+1)*512],
                            in_=zt[i % 2][:, :])])
                ztQ.wait_all(e)

            @b.tensor
            def _(e):
                for c in range(NCH):
                    for mt in range(NMT1):
                        i = c*NMT1 + mt
                        e.wait_ge(w1Q.sem(i).h, w1Q.ft(bw1, i))
                        if i >= 2:
                            cc, mm_ = divmod(i - 2, NMT1)
                            e.wait_ge(ctx.evs.h, b_evs + cc*NMT1 + mm_ + 1)
                        for kt in range(NKT):
                            lhs = w1t[i % 2][:, kt*128:(kt+1)*128]
                            e.ldweights(lhs)
                            ins = e.matmul(out=ps1[mt % 2][:, :], lhsT=lhs,
                                           rhs=hbuf[:, kt*T+c*512: kt*T+c*512+512],
                                           start=(kt == 0), stop=(kt == NKT-1))
                        ins.then_inc(ctx.mm.h, 1)
                    for mt in (range(NMT2) if do_fc2 else []):
                        i = c*NMT2 + mt
                        e.wait_ge(w2Q.sem(i).h, w2Q.ft(bw2, i))
                        e.wait_ge(ctx.evs.h, b_evs + (c+1)*NMT1)
                        if i >= 2:
                            e.wait_ge(ctx.evv.h, b_evv + (i - 1))
                        for kt in range(32):
                            lhs = w2t[i % 2][:, kt*128:(kt+1)*128]
                            e.ldweights(lhs)
                            ins = e.matmul(out=ps2[mt % 2][:, :], lhsT=lhs,
                                           rhs=ych[:, kt*512:(kt+1)*512],
                                           start=(kt == 0), stop=(kt == 31))
                        ins.then_inc(ctx.mm.h, 1)
                ctx.mm.inc(NCH*(per_chunk if do_fc2 else NMT1))

            @b.vector
            def _(e):
                if not do_fc2:
                    return
                for c in range(NCH):
                    for mt in range(NMT2):
                        i = c*NMT2 + mt
                        e.wait_ge(ctx.mm.h, b_mm + c*per_chunk + NMT1 + mt + 1)
                        e.wait_ge(rrQ.sem(i).h, rrQ.ft(brr, i))
                        if i >= 2:
                            e.wait_ge(ztQ.sem(i).h, ztQ.ft(bzt, i - 2))
                        e.scalar_tensor_tensor(out=zt[i % 2][:, :], in0=ps2[mt % 2][:, :],
                                               scalar=0.5, in1=rrt[i % 2][:, :],
                                               op0=ALU.mult, op1=ALU.add)
                        e.tensor_scalar(out=zt[i % 2][:, :], in0=zt[i % 2][:, :],
                                        scalar1=b2h[:, mt:mt+1], scalar2=None,
                                        op0=ALU.add).then_inc(ctx.evv.h, 1)
                ctx.evv.inc(NCH*NMT2)

        if upto == "fc1":
            with ctx.block("p2_ydbg") as b:
                @b.gpsimd
                def _(e):
                    e.dma_start(out=H["ydbg"][:, :], in_=ych[:, :]).then_inc(ctx.st.h, 16)
                    ctx.st.inc(16)
                    e.wait_ge(ctx.st.h, ctx.st.n)



def emit_attn(ctx, H, l1, upto="all"):
    nc = ctx.nc
    es = ExitStack()
    with es:
        SB = lambda *a: es.enter_context(nc.sbuf_tensor(*a))
        PS = lambda *a: es.enter_context(nc.psum_tensor(*a))
        NQS = NTT
        NKB = S // 128
        WIN = {qs: [4*qs + j for j in range(4)]
               for qs in range(NQS)} if not l1 else {qs: [] for qs in range(NQS)}

        lnfull = SB("at_ln", [128, NKT*S], BF16)
        if not l1:
            cmb = SB("at_cm", [128, 8*512], BF16)
            biasMt = SB("at_bm", [128, 128], F32)
        else:
            alibiT = SB("at_al", [128, 16*NKB], F32)
            iotaKC = SB("at_io", [128, NKB], F32)
            slv = SB("at_sl", [128, 16], F32)
        ebc = SB("at_ebc", [1, 128], F32)

        ownoff = (lambda kt: kt*S)
        othoff = (lambda kt: kt*S + T)
        for part in ("own", "oth"):
            ies = ExitStack()
            with ies:
                ISB = lambda *a: ies.enter_context(nc.sbuf_tensor(*a))
                xb = ISB(f"at_x{part}", [128, NKT*T], F32)
                lnb = ISB(f"at_lnb{part}", [128, 2*T], F32)
                bc = ISB(f"at_bc{part}", [128, 2*T], F32)
                rt = [ISB(f"at_rt{part}{i}", [128, 512], F32) for i in range(2)]
                src = H["xow"] if part == "own" else H["xot"]
                with ctx.block(f"at_ld{part}") as b:
                    @b.sync
                    def _(e):
                        e.dma_start(out=xb[:, :].rearrange("p (k t) -> p k t", k=NKT),
                                    in_=src.rearrange("(k p) t -> p k t", p=128)).then_inc(ctx.ld.h, 16)
                        ctx.ld.inc(16)
                        if part == "own" and not l1:
                            e.dma_start(out=biasMt[:, :], in_=H["biasM"][:, :]).then_inc(ctx.ld.h, 16)
                            ctx.ld.inc(16)
                        if part == "own" and l1:
                            e.dma_start(out=iotaKC[:, :], in_=H["iotaK"][:, :]).then_inc(ctx.ld.h, 16)
                            ctx.ld.inc(16)
                        e.wait_ge(ctx.ld.h, ctx.ld.n)

                emit_ln_stats(ctx, xb, lnb, f"at_st{part}")
                emit_ln_derive(ctx, lnb, bc, f"at_dv{part}", with_r=False)

                off = ownoff if part == "own" else othoff
                rQ = ctx.q("qGB")
                br = rQ.snap()
                b_evv = ctx.evv.n
                with ctx.block(f"at_ap{part}") as b:
                    @b.vector
                    def _(e):
                        for kt in range(NKT):
                            for c in range(NTT):
                                i = kt*NTT + c
                                sl = slice(kt*T + c*512, kt*T + c*512 + 512)
                                dsl = slice(off(kt) + c*512, off(kt) + c*512 + 512)
                                bcs = lambda j: slice(j*T + c*512, j*T + c*512 + 512)
                                if part == "own" and not l1:
                                    if i >= 2:
                                        e.wait_ge(rQ.sem(i).h, rQ.ft(br, i - 2))
                                e.tensor_tensor(out=lnfull[:, dsl], in0=xb[:, sl], in1=bc[:, bcs(0)], op=ALU.mult)
                                e.tensor_tensor(out=lnfull[:, dsl], in0=lnfull[:, dsl], in1=bc[:, bcs(1)], op=ALU.subtract)
                                if part == "own" and not l1:
                                    e.tensor_tensor(out=rt[i % 2][:, :], in0=xb[:, sl], in1=lnfull[:, dsl],
                                                    op=ALU.add).then_inc(ctx.evv.h, 1)
                        if part == "own" and not l1:
                            ctx.evv.inc(NKT*NTT)

                    if part == "own" and l1:
                        @b.gpsimd
                        def _(e):
                            gq0 = ctx.q("qGA")
                            gq0.issue(e, 0, [e.dma_start(out=slv[:, :],
                                             in_=H["slopes"][0:1, :].partition_broadcast(128))])
                            gq0.wait_all(e)
                    if part == "own" and not l1:
                        @b.gpsimd
                        def _(e):
                            for kt in range(NKT):
                                for c in range(NTT):
                                    i = kt*NTT + c
                                    e.wait_ge(ctx.evv.h, b_evv + i + 1)
                                    rQ.issue(e, i, [e.dma_start(
                                        out=H["rdr"][kt*128:(kt+1)*128, c*512:(c+1)*512],
                                        in_=rt[i % 2][:, :])])
                            rQ.wait_all(e)

        if upto == "ln":
            with ctx.block("at_lndbg") as b:
                @b.gpsimd
                def _(e):
                    e.dma_start(out=H["lndbg"][:, :], in_=lnfull[:, :]).then_inc(ctx.st.h, 16)
                    ctx.st.inc(16)
                    e.wait_ge(ctx.st.h, ctx.st.n)
            return

        ctxall = SB("at_ctx", [128, 8*T], BF16)
        s_ps = [PS(f"at_s{i}", [128, 512], F32) for i in range(4)]
        c_ps = [PS(f"at_c{i}", [128, 512], F32) for i in range(2)]
        a_ps = [PS(f"at_a{i}", [128, 512], F32) for i in range(2)]
        bes = ExitStack()
        bes.__enter__()
        BSB = lambda *a: bes.enter_context(nc.sbuf_tensor(*a))
        wq = BSB("at_wq", [128, NKT*1024], BF16)
        wk = BSB("at_wk", [128, NKT*1024], BF16)
        wv = BSB("at_wv", [128, NKT*1024], BF16)
        qhp = BSB("at_q", [128, T], BF16)
        khp = BSB("at_k", [128, S], BF16)
        vg = BSB("at_vg", [128, NKB*130], BF16)
        pA = [BSB(f"at_pA{i}", [128, 512], BF16) for i in range(2)]
        pB = [BSB(f"at_pB{i}", [128, 512], BF16) for i in range(2)]
        rs2 = BSB("at_rs", [1, 1024], F32)
        rinv = BSB("at_ri", [1, 1024], F32)
        bcv = BSB("at_bcv", [128, 512], F32)
        if l1:
            bvb = BSB("at_bvb", [128, 1024], F32)
            bq_sb = BSB("at_bq", [128, NKT], F32)
            bk_sb = BSB("at_bk", [128, NKT], F32)

        with ctx.block("at_wld") as b:
            @b.gpsimd
            def _(e):
                gq = ctx.q("qGA")
                gq.issue(e, 0, [e.dma_start(out=wq[:, :].rearrange("p (k c) -> p k c", k=NKT),
                                            in_=H["wq"].rearrange("(k p) c -> p k c", p=128))])
                gq.issue(e, 1, [e.dma_start(out=wk[:, :].rearrange("p (k c) -> p k c", k=NKT),
                                            in_=H["wk"].rearrange("(k p) c -> p k c", p=128))])
                gq.issue(e, 0, [e.dma_start(out=wv[:, :].rearrange("p (k c) -> p k c", k=NKT),
                                            in_=H["wv"].rearrange("(k p) c -> p k c", p=128))])
                if l1:
                    gq.issue(e, 1, [e.dma_start(out=bvb[:, :], in_=H["bv"][0:1, :].partition_broadcast(128))])
                    gq.issue(e, 0, [e.dma_start(out=bq_sb[:, :], in_=H["bq"][:, :])])
                    gq.issue(e, 1, [e.dma_start(out=bk_sb[:, :], in_=H["bk"][:, :])])
                gq.wait_all(e)

            @b.vector
            def _(e):
                e.memset(ebc[:, :], 1.0)
                if l1:
                    for h_ in range(16):
                        e.tensor_scalar(out=alibiT[:, h_*NKB:(h_+1)*NKB], in0=iotaKC[:, :],
                                        scalar1=slv[:, h_:h_+1], scalar2=None, op0=ALU.mult)

        for hp in range(8):
            b_mm, b_evs = ctx.mm.n, ctx.evs.n
            with ctx.block(f"at_qk{hp}") as b:
                @b.tensor
                def _(e):
                    nmil = 0
                    for which, wmat, nch in (("q", wq, NTT), ("k", wk, S//512)):
                        for c in range(nch):
                            if nmil >= 2:
                                e.wait_ge(ctx.evs.h, b_evs + nmil - 1)
                            for kt in range(NKT):
                                lhs = wmat[:, kt*1024 + hp*128: kt*1024 + hp*128 + 128]
                                e.ldweights(lhs)
                                ins = e.matmul(out=a_ps[nmil % 2][:, :], lhsT=lhs,
                                               rhs=lnfull[:, (ownoff(kt) if which == "q" else kt*S) + c*512:
                                                          (ownoff(kt) if which == "q" else kt*S) + c*512 + 512],
                                               start=(kt == 0), stop=(kt == NKT-1))
                            ins.then_inc(ctx.mm.h, 1)
                            nmil += 1
                    ctx.mm.inc(nmil)

                @b.scalar
                def _(e):
                    nmil = 0
                    for which, nch in (("q", NTT), ("k", S//512)):
                        for c in range(nch):
                            e.wait_ge(ctx.mm.h, b_mm + nmil + 1)
                            dst = qhp if which == "q" else khp
                            if l1:
                                bsb = bq_sb if which == "q" else bk_sb
                                e.activation(out=dst[:, c*512:(c+1)*512], in_=a_ps[nmil % 2][:, :],
                                             func=AF.Identity, bias=bsb[:, hp:hp+1]).then_inc(ctx.evs.h, 1)
                            else:
                                e.activation(out=dst[:, c*512:(c+1)*512], in_=a_ps[nmil % 2][:, :],
                                             func=AF.Identity).then_inc(ctx.evs.h, 1)
                            nmil += 1
                    ctx.evs.inc(nmil)

            b_mm, b_evv = ctx.mm.n, ctx.evv.n
            with ctx.block(f"at_v{hp}") as b:
                @b.vector
                def _(e):
                    e.memset(vg[:, :], 1.0)
                    for tt in range(NKB):
                        e.wait_ge(ctx.mm.h, b_mm + tt + 1)
                        if l1:
                            e.tensor_tensor(out=vg[:, tt*130: tt*130 + 64], in0=a_ps[tt % 2][:, 0:64],
                                            in1=bvb[:, hp*128: hp*128 + 64], op=ALU.add)
                            e.tensor_tensor(out=vg[:, tt*130 + 65: tt*130 + 129],
                                            in0=a_ps[tt % 2][:, 64:128],
                                            in1=bvb[:, hp*128 + 64: hp*128 + 128],
                                            op=ALU.add).then_inc(ctx.evv.h, 1)
                        else:
                            e.tensor_copy(out=vg[:, tt*130: tt*130 + 64], in_=a_ps[tt % 2][:, 0:64])
                            e.tensor_copy(out=vg[:, tt*130 + 65: tt*130 + 129],
                                          in_=a_ps[tt % 2][:, 64:128]).then_inc(ctx.evv.h, 1)
                    ctx.evv.inc(NKB)

                @b.tensor
                def _(e):
                    for tt in range(NKB):
                        if tt >= 2:
                            e.wait_ge(ctx.evv.h, b_evv + tt - 1)
                        for kt in range(NKT):
                            lhs = lnfull[:, kt*S + tt*128: kt*S + tt*128 + 128]
                            e.ldweights(lhs)
                            ins = e.matmul(out=a_ps[tt % 2][:, 0:128], lhsT=lhs,
                                           rhs=wv[:, kt*1024 + hp*128: kt*1024 + hp*128 + 128],
                                           start=(kt == 0), stop=(kt == NKT-1))
                        ins.then_inc(ctx.mm.h, 1)
                    ctx.mm.inc(NKB)


            for qs in range(NQS):
                masked = WIN[qs] if not l1 else []
                b_pp, b_pe, b_pv, b_pc = ctx.pp.n, ctx.pe.n, ctx.pv.n, ctx.pc.n
                if masked:
                    cmQ = ctx.q("qSY")
                    bcm = cmQ.snap()
                with ctx.block(f"at_at{hp}_{qs}") as b:
                    if masked:
                        @b.sync
                        def _(e):
                            cmQ.issue(e, 0, [e.dma_start(
                                out=cmb[:, 0:4*512].rearrange("p (g t) -> p g t", g=4),
                                in_=H["cmask"][qs].rearrange("b p t -> p b t"))])

                    @b.tensor
                    def _(e):
                        def s_mms(kb):
                            if kb >= 2:
                                e.wait_ge(ctx.pe.h, b_pe + kb - 1)
                            sa = s_ps[2*(kb % 2)]
                            sb_ = s_ps[2*(kb % 2) + 1]
                            for half2, out_, kbuf in ((0, sa, khp[0:64, :]), (1, sb_, khp[64:128, :])):
                                qv = qhp[64*half2:64*half2 + 64, qs*512:(qs+1)*512]
                                for sub in range(2):
                                    tp = (64*half2, 64*sub)
                                    lhs = khp[64*half2:64*half2 + 64,
                                              kb*128 + 64*sub: kb*128 + 64*sub + 64]
                                    e.ldweights(lhs, tile_position=tp)
                                    ins = e.matmul(out=out_[64*sub:64*sub + 64, :], lhsT=lhs,
                                                   rhs=qv, start=True, stop=True,
                                                   tile_position=tp)
                            ins.then_inc(ctx.pp.h, 1)

                        s_mms(0)
                        s_mms(1)
                        for kb in range(NKB):
                            e.wait_ge(ctx.pe.h, b_pe + kb + 1)
                            if kb in masked:
                                e.wait_ge(ctx.pv.h, b_pv + masked.index(kb) + 1)
                            lhsA = vg[:, kb*130: kb*130 + 65]
                            e.ldweights(lhsA)
                            e.matmul(out=c_ps[0][0:65, :], lhsT=lhsA, rhs=pA[kb % 2][:, :],
                                     start=(kb == 0), stop=(kb == NKB-1))
                            lhsB = vg[:, kb*130 + 65: kb*130 + 130]
                            e.ldweights(lhsB)
                            ins = e.matmul(out=c_ps[1][0:65, :], lhsT=lhsB, rhs=pB[kb % 2][:, :],
                                           start=(kb == 0), stop=(kb == NKB-1))
                            ins.then_inc(ctx.pc.h, 1)
                            if kb + 2 < NKB:
                                s_mms(kb + 2)
                        ctx.pp.inc(NKB)
                        ctx.pc.inc(NKB)

                    @b.scalar
                    def _(e):
                        for kb in range(NKB):
                            e.wait_ge(ctx.pp.h, b_pp + kb + 1)
                            if kb >= 2:
                                e.wait_ge(ctx.pc.h, b_pc + kb - 1)
                            if l1:
                                biasA = alibiT[:, (2*hp)*NKB + kb: (2*hp)*NKB + kb + 1]
                                biasB = alibiT[:, (2*hp+1)*NKB + kb: (2*hp+1)*NKB + kb + 1]
                            else:
                                col = qs*32 + kb
                                biasA = biasB = biasMt[:, col:col+1]
                            e.activation(out=pA[kb % 2][:, :], in_=s_ps[2*(kb % 2)][:, :],
                                         func=AF.Exp, scale=0.125, bias=biasA)
                            ins = e.activation(out=pB[kb % 2][:, :], in_=s_ps[2*(kb % 2) + 1][:, :],
                                               func=AF.Exp, scale=0.125, bias=biasB)
                            ins.then_inc(ctx.pe.h, 1)
                        ctx.pe.inc(NKB)

                    @b.vector
                    def _(e):
                        if masked:
                            e.wait_ge(cmQ.sem(0).h, cmQ.ft(bcm, 0))
                        for j, kb in enumerate(masked):
                            e.wait_ge(ctx.pe.h, b_pe + kb + 1)
                            moff = j*512
                            e.tensor_tensor(out=pA[kb % 2][:, :], in0=pA[kb % 2][:, :],
                                            in1=cmb[:, moff:moff+512], op=ALU.mult)
                            ins = e.tensor_tensor(out=pB[kb % 2][:, :], in0=pB[kb % 2][:, :],
                                                  in1=cmb[:, moff:moff+512], op=ALU.mult)
                            ins.then_inc(ctx.pv.h, 1)
                        ctx.pv.inc(len(masked))

                b_pc, b_evv, b_mm = ctx.pc.n, ctx.evv.n, ctx.mm.n
                with ctx.block(f"at_nm{hp}_{qs}") as b:
                    @b.vector
                    def _(e):
                        e.tensor_copy(out=rs2[0:1, 0:512], in_=c_ps[0][64:65, :])
                        e.tensor_copy(out=rs2[0:1, 512:1024], in_=c_ps[1][64:65, :])
                        e.reciprocal(out=rinv[0:1, :], in_=rs2[0:1, :]).then_inc(ctx.evv.h, 1)
                        ctx.evv.inc()
                        e.wait_ge(ctx.mm.h, b_mm + 1)
                        e.tensor_copy(out=bcv[:, :], in_=a_ps[0][:, :]).then_inc(ctx.evv.h, 1)
                        ctx.evv.inc()
                        csl = slice(hp*T + qs*512, hp*T + qs*512 + 512)
                        e.tensor_tensor(out=ctxall[0:64, csl], in0=c_ps[0][0:64, :],
                                        in1=bcv[0:64, :], op=ALU.mult)
                        e.tensor_tensor(out=ctxall[64:128, csl], in0=c_ps[1][0:64, :],
                                        in1=bcv[64:128, :], op=ALU.mult)

                    @b.tensor
                    def _(e):
                        e.wait_ge(ctx.evv.h, b_evv + 1)
                        e.matmul(out=a_ps[0][0:64, :], lhsT=ebc[:, 0:64],
                                 rhs=rinv[0:1, 0:512], start=True, stop=True)
                        ins = e.matmul(out=a_ps[0][64:128, :], lhsT=ebc[:, 0:64],
                                       rhs=rinv[0:1, 512:1024], start=True, stop=True)
                        ins.then_inc(ctx.mm.h, 1)
                        ctx.mm.inc()

        if upto == "ctx":
            b_st = ctx.st.n
            with ctx.block("at_cdbg") as b:
                @b.gpsimd
                def _(e):
                    e.dma_start(out=H["ctxdbg"][:, :], in_=ctxall[:, :]).then_inc(ctx.st.h, 16)
                    ctx.st.inc(16)
                    e.dma_start(out=H["pdbg"][:, 0:512], in_=pA[1][:, :]).then_inc(ctx.st.h, 16)
                    ctx.st.inc(16)
                    e.dma_start(out=H["pdbg"][:, 512:1024], in_=pA[0][:, :]).then_inc(ctx.st.h, 16)
                    ctx.st.inc(16)
                    e.dma_start(out=H["kdbg"][:, :], in_=khp[:, :]).then_inc(ctx.st.h, 16)
                    ctx.st.inc(16)
                    e.dma_start(out=H["qdbg"][:, :], in_=qhp[:, :]).then_inc(ctx.st.h, 16)
                    ctx.st.inc(16)
                    e.dma_start(out=H["vdbg"][:, :], in_=vg[:, :]).then_inc(ctx.st.h, 16)
                    ctx.st.inc(16)
                    e.wait_ge(ctx.st.h, ctx.st.n)
            bes.__exit__(None, None, None)
            return
        bes.__exit__(None, None, None)

        wo = SB("at_wo", [128, 8*1024], BF16)
        rrt = [SB(f"at_rr{i}", [128, 512], F32) for i in range(2)]
        zt = [SB(f"at_z{i}", [128, 512], F32) for i in range(2)]
        bo_sb = SB("at_bo", [128, NKT], F32)
        with ctx.block("at_wold") as b:
            @b.gpsimd
            def _(e):
                gq = ctx.q("qGA")
                gq.issue(e, 0, [e.dma_start(out=wo[:, :].rearrange("p (k c) -> p k c", k=8),
                                            in_=H["wo"].rearrange("(k p) c -> p k c", p=128))])
                if l1:
                    gq.issue(e, 1, [e.dma_start(out=bo_sb[:, :], in_=H["bo"][:, :])])
                gq.wait_all(e)

        rrQ, ztQ = ctx.q("qGA"), ctx.q("qGB")
        brr, bzt = rrQ.snap(), ztQ.snap()
        b_mm, b_evv = ctx.mm.n, ctx.evv.n
        with ctx.block("at_out") as b:
            @b.tensor
            def _(e):
                for c in range(NTT):
                    for mt in range(NKT):
                        i = c*NKT + mt
                        if i >= 2:
                            e.wait_ge(ctx.evv.h, b_evv + i - 1)
                        for hp in range(8):
                            lhs = wo[:, hp*1024 + mt*128: hp*1024 + mt*128 + 128]
                            e.ldweights(lhs)
                            ins = e.matmul(out=a_ps[i % 2][:, :], lhsT=lhs,
                                           rhs=ctxall[:, hp*T + c*512: hp*T + c*512 + 512],
                                           start=(hp == 0), stop=(hp == 7))
                        ins.then_inc(ctx.mm.h, 1)
                ctx.mm.inc(NTT*NKT)

            @b.gpsimd
            def _(e):
                for c in range(NTT):
                    for mt in range(NKT):
                        i = c*NKT + mt
                        if i >= 2:
                            e.wait_ge(ctx.evv.h, b_evv + i - 1)
                        rrQ.issue(e, i, [e.dma_start(
                            out=rrt[i % 2][:, :],
                            in_=H["rdr" if not l1 else "xow"][mt*128:(mt+1)*128, c*512:(c+1)*512])])
                        e.wait_ge(ctx.evv.h, b_evv + i + 1)
                        ztQ.issue(e, i, [e.dma_start(
                            out=H["xout"][mt*128:(mt+1)*128, c*512:(c+1)*512],
                            in_=zt[i % 2][:, :])])
                ztQ.wait_all(e)

            @b.vector
            def _(e):
                for c in range(NTT):
                    for mt in range(NKT):
                        i = c*NKT + mt
                        e.wait_ge(ctx.mm.h, b_mm + i + 1)
                        e.wait_ge(rrQ.sem(i).h, rrQ.ft(brr, i))
                        if i >= 2:
                            e.wait_ge(ztQ.sem(i).h, ztQ.ft(bzt, i - 2))
                        if l1:
                            e.tensor_scalar(out=zt[i % 2][:, :], in0=a_ps[i % 2][:, :],
                                            scalar1=bo_sb[:, mt:mt+1], scalar2=None, op0=ALU.add)
                            ins = e.tensor_tensor(out=zt[i % 2][:, :], in0=zt[i % 2][:, :],
                                                  in1=rrt[i % 2][:, :], op=ALU.add)
                        else:
                            ins = e.tensor_tensor(out=zt[i % 2][:, :], in0=a_ps[i % 2][:, :],
                                                  in1=rrt[i % 2][:, :], op=ALU.add)
                        ins.then_inc(ctx.evv.h, 1)
                ctx.evv.inc(NTT*NKT)



def emit_p0(ctx, H):
    nc = ctx.nc
    es = ExitStack()
    with es:
        SB = lambda *a: es.enter_context(nc.sbuf_tensor(*a))
        PS = lambda *a: es.enter_context(nc.psum_tensor(*a))
        W1, W2 = 2048, 1024
        byt = [SB(f"p0_by{i}", [128, W1], U8) for i in range(2)]
        lo8 = SB("p0_lo8", [128, W1], U8)
        hi8 = SB("p0_hi8", [128, W1], U8)
        nib = SB("p0_nib", [128, 2*W1], BF16)
        t_m = SB("p0_m", [128, 2*W1], BF16)
        t_r1 = SB("p0_r1", [128, 2*W1], BF16)
        t_r2 = SB("p0_r2", [128, 2*W1], BF16)
        t_sg = SB("p0_sg", [128, 2*W1], BF16)
        val = [SB(f"p0_v{i}", [128, 2*W1], BF16) for i in range(2)]
        srow = SB("p0_srow", [32, W1], U8)
        spow = SB("p0_spow", [32, W1], BF16)
        spf = SB("p0_spf", [32, W1], F32)
        emat = SB("p0_em", [32, 4*128], BF16)
        sct = SB("p0_sct", [128, 4*W1], BF16)
        scp = [PS(f"p0_sc{i}", [128, 512], F32) for i in range(2)]

        ehost = np.zeros((32, 4*128), dtype=np.float32)
        for t_ in range(4):
            for p_ in range(128):
                ehost[8*t_ + p_//16, t_*128 + p_] = 1.0
        eh = nc.inline_tensor(ehost.astype(BF), name="p0_emc")

        with ctx.block("p0_pre") as b:
            @b.sync
            def _(e):
                e.dma_start(out=emat[:, :], in_=eh.ap()).then_inc(ctx.ld.h, 16)
                ctx.ld.inc(16)
                e.wait_ge(ctx.ld.h, ctx.ld.n)

        ldQ, stQ, scQ = ctx.q("qSY"), ctx.q("qGB"), ctx.q("qSC")
        for ex in range(4):
            for mat, Wd in (("1", W1), ("2", W2)):
                src, ssrc, dst = H[f"w{mat}b"], H[f"s{mat}u"], H[f"w{mat}dec"]
                nch = Wd // 512

                bsc = scQ.snap()
                b_evs, b_mm, b_evv = ctx.evs.n, ctx.mm.n, ctx.evv.n
                with ctx.block(f"p0_s{ex}{mat}") as b:
                    @b.scalar
                    def _(e):
                        scQ.issue(e, 0, [e.dma_start(out=srow[:, 0:Wd], in_=ssrc[ex])])
                        e.wait_ge(scQ.sem(0).h, scQ.ft(bsc, 0))
                        e.activation(out=spf[:, 0:Wd], in_=srow[:, 0:Wd], func=AF.Identity)
                        e.activation(out=spow[:, 0:Wd], in_=spf[:, 0:Wd], func=AF.Exp,
                                     scale=float(np.log(2.0)),
                                     bias=ctx.c(-127.0*float(np.log(2.0)), 32)).then_inc(ctx.evs.h, 1)
                        ctx.evs.inc()

                    @b.tensor
                    def _(e):
                        e.wait_ge(ctx.evs.h, b_evs + 1)
                        nmm = 0
                        for t_ in range(4):
                            for cch in range(nch):
                                if nmm >= 2:
                                    e.wait_ge(ctx.evv.h, b_evv + nmm - 1)
                                lhs = emat[:, t_*128:(t_+1)*128]
                                e.ldweights(lhs)
                                ins = e.matmul(out=scp[nmm % 2][:, :], lhsT=lhs,
                                               rhs=spow[:, cch*512:(cch+1)*512],
                                               start=True, stop=True)
                                ins.then_inc(ctx.mm.h, 1)
                                nmm += 1
                        ctx.mm.inc(nmm)

                    @b.vector
                    def _(e):
                        nmm = 0
                        for t_ in range(4):
                            for cch in range(nch):
                                e.wait_ge(ctx.mm.h, b_mm + nmm + 1)
                                e.tensor_copy(out=sct[:, t_*Wd + cch*512: t_*Wd + cch*512 + 512],
                                              in_=scp[nmm % 2][:, :]).then_inc(ctx.evv.h, 1)
                                nmm += 1
                        ctx.evv.inc(nmm)

                bld, bst = ldQ.snap(), stQ.snap()
                b_evv = ctx.evv.n
                with ctx.block(f"p0_d{ex}{mat}") as b:
                    @b.sync
                    def _(e):
                        for t_ in range(4):
                            if t_ >= 2:
                                e.wait_ge(ctx.evv.h, b_evv + t_ - 1)
                            ldQ.issue(e, t_, [e.dma_start(out=byt[t_ % 2][:, 0:Wd],
                                                          in_=src[ex, t_*128:(t_+1)*128, :])])

                    @b.vector
                    def _(e):
                        for t_ in range(4):
                            e.wait_ge(ldQ.sem(t_).h, ldQ.ft(bld, t_))
                            if t_ >= 2:
                                e.wait_ge(stQ.sem(t_).h, stQ.ft(bst, t_ - 2, per=32))
                            bt = byt[t_ % 2][:, 0:Wd]
                            e.tensor_scalar(out=lo8[:, 0:Wd], in0=bt, scalar1=15, scalar2=None, op0=ALU.bitwise_and)
                            e.tensor_scalar(out=hi8[:, 0:Wd], in0=bt, scalar1=4, scalar2=None, op0=ALU.logical_shift_right)
                            e.tensor_copy(out=nib[:, 0:Wd], in_=lo8[:, 0:Wd])
                            e.tensor_copy(out=nib[:, Wd:2*Wd], in_=hi8[:, 0:Wd])
                            n2 = slice(0, 2*Wd)
                            e.tensor_scalar(out=t_sg[:, n2], in0=nib[:, n2], scalar1=8.0, scalar2=8.0,
                                            op0=ALU.is_ge, op1=ALU.mult)
                            e.tensor_tensor(out=t_m[:, n2], in0=nib[:, n2], in1=t_sg[:, n2], op=ALU.subtract)
                            e.tensor_scalar(out=t_r1[:, n2], in0=t_m[:, n2], scalar1=-4.0, scalar2=0.0,
                                            op0=ALU.add, op1=ALU.max)
                            e.tensor_scalar(out=t_r2[:, n2], in0=t_m[:, n2], scalar1=-6.0, scalar2=0.0,
                                            op0=ALU.add, op1=ALU.max)
                            e.tensor_tensor(out=t_m[:, n2], in0=t_m[:, n2], in1=t_r1[:, n2], op=ALU.add)
                            e.scalar_tensor_tensor(out=t_m[:, n2], in0=t_m[:, n2], scalar=0.5, in1=t_r2[:, n2],
                                                   op0=ALU.mult, op1=ALU.add)
                            e.tensor_scalar(out=t_sg[:, n2], in0=t_sg[:, n2], scalar1=-0.25, scalar2=1.0,
                                            op0=ALU.mult, op1=ALU.add)
                            e.tensor_tensor(out=t_m[:, n2], in0=t_m[:, n2], in1=t_sg[:, n2], op=ALU.mult)
                            e.tensor_tensor(out=val[t_ % 2][:, 0:Wd], in0=t_m[:, 0:Wd],
                                            in1=sct[:, t_*Wd:(t_+1)*Wd], op=ALU.mult)
                            e.tensor_tensor(out=val[t_ % 2][:, Wd:2*Wd], in0=t_m[:, Wd:2*Wd],
                                            in1=sct[:, t_*Wd:(t_+1)*Wd],
                                            op=ALU.mult).then_inc(ctx.evv.h, 1)
                        ctx.evv.inc(4)

                    @b.gpsimd
                    def _(e):
                        for t_ in range(4):
                            e.wait_ge(ctx.evv.h, b_evv + t_ + 1)
                            stQ.issue(e, t_, [
                                e.dma_start(out=dst[ex, t_*128:(t_+1)*128, :], in_=val[t_ % 2][:, 0:Wd]),
                                e.dma_start(out=dst[ex, 512 + t_*128: 512 + (t_+1)*128, :],
                                            in_=val[t_ % 2][:, Wd:2*Wd])])
                        stQ.wait_all(e)
    return



def emit_p4(ctx, H, upto="all"):
    nc = ctx.nc
    es = ExitStack()
    with es:
        SB = lambda *a: es.enter_context(nc.sbuf_tensor(*a))
        PS = lambda *a: es.enter_context(nc.psum_tensor(*a))
        NE = 32

        lnx = SB("p4_lnx", [128, NKT*T], BF16)
        acc = SB("p4_acc", [128, NKT*T], BF16)
        lgall = SB("p4_lg", [128, 16*32], F32)
        routeA = SB("p4_ra", [128, 16*32], F32)
        routeT = SB("p4_rt", [32, T], F32)
        routeTb = SB("p4_rtb", [32, T], BF16)
        gbb = SB("p4_gbb", [128, 32], F32)
        nwsb = SB("p4_nw", [128, NKT], F32)
        idn = SB("p4_id", [128, 128], F32)
        onesq = SB("p4_on", [128, 128], BF16)

        idh = nc.inline_tensor(np.eye(128, dtype=np.float32), name="p4_idc")

        ies = ExitStack()
        with ies:
            ISB = lambda *a: ies.enter_context(nc.sbuf_tensor(*a))
            xb = ISB("p4_x", [128, NKT*T], F32)
            lnb = ISB("p4_lnb", [128, 2*T], F32)
            bc = ISB("p4_bc", [128, 2*T], F32)
            rt = [ISB(f"p4_rt{i}", [128, 512], F32) for i in range(2)]
            with ctx.block("p4_load") as b:
                @b.sync
                def _(e):
                    e.dma_start(out=xb[:, :].rearrange("p (k t) -> p k t", k=NKT),
                                in_=H["xin"].rearrange("(k p) t -> p k t", p=128)).then_inc(ctx.ld.h, 16)
                    ctx.ld.inc(16)
                    e.dma_start(out=nwsb[:, :], in_=H["nw"][:, :]).then_inc(ctx.ld.h, 16)
                    ctx.ld.inc(16)
                    e.dma_start(out=idn[:, :], in_=idh.ap()).then_inc(ctx.ld.h, 16)
                    ctx.ld.inc(16)
                    e.wait_ge(ctx.ld.h, ctx.ld.n)

                @b.gpsimd
                def _(e):
                    gq = ctx.q("qGA")
                    gq.issue(e, 0, [e.dma_start(out=gbb[:, :],
                                                in_=H["gateb"][0:1, :].partition_broadcast(128))])
                    gq.wait_all(e)

                @b.vector
                def _(e):
                    e.memset(onesq[:, :], 1.0)
                    e.memset(acc[:, :], 0.0)

            emit_ln_stats(ctx, xb, lnb, "p4ln")
            emit_ln_derive(ctx, lnb, bc, "p4drv", with_r=False)

            rQ = ctx.q("qGB")
            br = rQ.snap()
            b_evv = ctx.evv.n
            with ctx.block("p4_ap") as b:
                @b.vector
                def _(e):
                    for kt in range(NKT):
                        for c in range(NTT):
                            i = kt*NTT + c
                            sl = slice(kt*T + c*512, kt*T + c*512 + 512)
                            bcs = lambda j: slice(j*T + c*512, j*T + c*512 + 512)
                            if i >= 2:
                                e.wait_ge(rQ.sem(i).h, rQ.ft(br, i - 2))
                            e.tensor_tensor(out=lnx[:, sl], in0=xb[:, sl], in1=bc[:, bcs(0)], op=ALU.mult)
                            e.tensor_tensor(out=lnx[:, sl], in0=lnx[:, sl], in1=bc[:, bcs(1)], op=ALU.subtract)
                            e.tensor_tensor(out=rt[i % 2][:, :], in0=xb[:, sl], in1=lnx[:, sl],
                                            op=ALU.add).then_inc(ctx.evv.h, 1)
                    ctx.evv.inc(NKT*NTT)

                @b.gpsimd
                def _(e):
                    for kt in range(NKT):
                        for c in range(NTT):
                            i = kt*NTT + c
                            e.wait_ge(ctx.evv.h, b_evv + i + 1)
                            rQ.issue(e, i, [e.dma_start(
                                out=H["rdr"][kt*128:(kt+1)*128, c*512:(c+1)*512],
                                in_=rt[i % 2][:, :])])
                    rQ.wait_all(e)

        ies = ExitStack()
        with ies:
            ISB = lambda *a: ies.enter_context(nc.sbuf_tensor(*a))
            sqb = [ISB(f"p4_sq{i}", [128, 512], BF16) for i in range(2)]
            rmsb = ISB("p4_rms", [128, T], F32)
            msp = [ies.enter_context(nc.psum_tensor(f"p4_ms{i}", [128, 512], F32)) for i in range(2)]
            b_evs, b_mm, b_evv = ctx.evs.n, ctx.mm.n, ctx.evv.n
            with ctx.block("p4_rms") as b:
                @b.scalar
                def _(e):
                    for c in range(NTT):
                        for kt in range(NKT):
                            i = c*NKT + kt
                            if i >= 2:
                                e.wait_ge(ctx.mm.h, b_mm + (i - 1))
                            e.activation(out=sqb[i % 2][:, :], in_=lnx[:, kt*T+c*512: kt*T+c*512+512],
                                         func=AF.Square).then_inc(ctx.evs.h, 1)
                    ctx.evs.inc(NTT*NKT)
                    e.wait_ge(ctx.evv.h, b_evv + NTT)
                    e.activation(out=rmsb[:, :], in_=rmsb[:, :], func=AF.Sqrt,
                                 bias=ctx.c(EPS)).then_inc(ctx.evs.h, 1)
                    ctx.evs.inc()

                @b.tensor
                def _(e):
                    for c in range(NTT):
                        if c >= 2:
                            e.wait_ge(ctx.evv.h, b_evv + (c - 1))
                        for kt in range(NKT):
                            i = c*NKT + kt
                            e.wait_ge(ctx.evs.h, b_evs + i + 1)
                            e.ldweights(onesq[:, :])
                            ins = e.matmul(out=msp[c % 2][:, :], lhsT=onesq[:, :],
                                           rhs=sqb[i % 2][:, :],
                                           start=(kt == 0), stop=(kt == NKT-1))
                            ins.then_inc(ctx.mm.h, 1)
                    ctx.mm.inc(NTT*NKT)

                @b.vector
                def _(e):
                    for c in range(NTT):
                        e.wait_ge(ctx.mm.h, b_mm + (c + 1)*NKT)
                        e.tensor_scalar(out=rmsb[:, c*512:(c+1)*512], in0=msp[c % 2][:, :],
                                        scalar1=1.0/D, scalar2=None, op0=ALU.mult).then_inc(ctx.evv.h, 1)
                    ctx.evv.inc(NTT)
                    e.wait_ge(ctx.evs.h, b_evs + NTT*NKT + 1)
                    e.reciprocal(out=rmsb[:, :], in_=rmsb[:, :])
                    for kt in range(NKT):
                        sl = slice(kt*T, (kt+1)*T)
                        e.tensor_tensor(out=lnx[:, sl], in0=lnx[:, sl], in1=rmsb[:, :], op=ALU.mult)
                        e.tensor_scalar(out=lnx[:, sl], in0=lnx[:, sl],
                                        scalar1=nwsb[:, kt:kt+1], scalar2=None, op0=ALU.mult)

        ies = ExitStack()
        with ies:
            ISB = lambda *a: ies.enter_context(nc.sbuf_tensor(*a))
            gwsb = ISB("p4_gw", [128, NKT*32], BF16)
            lps = [ies.enter_context(nc.psum_tensor(f"p4_lp{i}", [128, 128], F32)) for i in range(2)]
            b_ldg = ctx.ldg.n
            b_mm, b_evv, b_evs = ctx.mm.n, ctx.evv.n, ctx.evs.n
            with ctx.block("p4_gate") as b:
                @b.gpsimd
                def _(e):
                    e.dma_start(out=gwsb[:, :].rearrange("p (k e) -> p k e", k=NKT),
                                in_=H["gwT"].rearrange("(k p) e -> p k e", p=128)).then_inc(ctx.ldg.h, 16)
                    ctx.ldg.inc(16)

                @b.tensor
                def _(e):
                    e.wait_ge(ctx.ldg.h, b_ldg + 16)
                    for tt in range(16):
                        if tt >= 2:
                            e.wait_ge(ctx.evv.h, b_evv + tt - 1)
                        for kt in range(NKT):
                            lhs = lnx[:, kt*T + tt*128: kt*T + tt*128 + 128]
                            e.ldweights(lhs)
                            ins = e.matmul(out=lps[tt % 2][:, 0:32], lhsT=lhs,
                                           rhs=gwsb[:, kt*32:(kt+1)*32],
                                           start=(kt == 0), stop=(kt == NKT-1))
                        ins.then_inc(ctx.mm.h, 1)
                    ctx.mm.inc(16)

                @b.vector
                def _(e):
                    for tt in range(16):
                        e.wait_ge(ctx.mm.h, b_mm + tt + 1)
                        e.tensor_tensor(out=lgall[:, tt*32:(tt+1)*32], in0=lps[tt % 2][:, 0:32],
                                        in1=gbb[:, :], op=ALU.add).then_inc(ctx.evv.h, 1)
                    ctx.evv.inc(16)

            cur4 = ISB("p4_cur", [128, 512], F32)
            eqb4 = ISB("p4_eqb", [128, 512], F32)
            kpb4 = ISB("p4_kpb", [128, 512], F32)
            sc4 = ISB("p4_sc4", [128, 112], F32)
            for j in range(4):
                src = lgall if j == 0 else cur4
                with ctx.block(f"p4_tkr{j}") as b:
                    @b.vector
                    def _(e, j=j, src=src):
                        for tt in range(16):
                            e.reduce_max(out=sc4[:, 16*j+tt:16*j+tt+1],
                                         in_=src[:, tt*32:(tt+1)*32], axis=mybir.AxisListType.X)
                with ctx.block(f"p4_tkm{j}") as b:
                    @b.vector
                    def _(e, j=j, src=src):
                        for tt in range(16):
                            sl = slice(tt*32, (tt+1)*32)
                            e.tensor_scalar(out=eqb4[:, sl], in0=src[:, sl],
                                            scalar1=sc4[:, 16*j+tt:16*j+tt+1], scalar2=-1e30,
                                            op0=ALU.is_ge, op1=ALU.mult)
                        e.tensor_tensor(out=cur4[:, :], in0=src[:, :], in1=eqb4[:, :], op=ALU.add)
            with ctx.block("p4_tkf1") as b:
                @b.vector
                def _(e):
                    e.tensor_scalar(out=sc4[:, 64:80], in0=sc4[:, 0:16], scalar1=-1.0,
                                    scalar2=None, op0=ALU.mult)
                    e.tensor_tensor(out=kpb4[:, :], in0=lgall[:, :], in1=cur4[:, :], op=ALU.is_gt)
            with ctx.block("p4_tkf2") as b:
                @b.scalar
                def _(e):
                    for tt in range(16):
                        e.activation(out=routeA[:, tt*32:(tt+1)*32], in_=lgall[:, tt*32:(tt+1)*32],
                                     func=AF.Exp, bias=sc4[:, 64+tt:64+tt+1])
            with ctx.block("p4_tkf3") as b:
                @b.vector
                def _(e):
                    e.tensor_tensor(out=routeA[:, :], in0=routeA[:, :], in1=kpb4[:, :], op=ALU.mult)
            with ctx.block("p4_tkf4") as b:
                @b.vector
                def _(e):
                    for tt in range(16):
                        e.reduce_sum(out=sc4[:, 80+tt:80+tt+1], in_=routeA[:, tt*32:(tt+1)*32],
                                     axis=mybir.AxisListType.X)
            with ctx.block("p4_tkf5") as b:
                @b.vector
                def _(e):
                    e.reciprocal(out=sc4[:, 96:112], in_=sc4[:, 80:96])
            with ctx.block("p4_tkf6") as b:
                @b.vector
                def _(e):
                    for tt in range(16):
                        sl = slice(tt*32, (tt+1)*32)
                        e.tensor_scalar(out=routeA[:, sl], in0=routeA[:, sl],
                                        scalar1=sc4[:, 96+tt:96+tt+1], scalar2=None, op0=ALU.mult)

            b_mm, b_evv = ctx.mm.n, ctx.evv.n
            with ctx.block("p4_rtp") as b:
                @b.tensor
                def _(e):
                    for tt in range(16):
                        if tt >= 2:
                            e.wait_ge(ctx.evv.h, b_evv + tt - 1)
                        ins = e.transpose(out=lps[tt % 2][0:32, :], in_=routeA[:, tt*32:(tt+1)*32],
                                          identity=idn[:, :])
                        ins.then_inc(ctx.mm.h, 1)
                    ctx.mm.inc(16)

                @b.vector
                def _(e):
                    for tt in range(16):
                        e.wait_ge(ctx.mm.h, b_mm + tt + 1)
                        e.tensor_copy(out=routeT[:, tt*128:(tt+1)*128], in_=lps[tt % 2][0:32, :])
                        e.tensor_copy(out=routeTb[:, tt*128:(tt+1)*128],
                                      in_=lps[tt % 2][0:32, :]).then_inc(ctx.evv.h, 1)
                    ctx.evv.inc(16)

            with ctx.block("p4_rost") as b:
                @b.gpsimd
                def _(e):
                    gq = ctx.q("qGB")
                    gq.issue(e, 0, [e.dma_start(out=H["rodr"][:, :], in_=routeT[:, :])])
                    if upto == "route":
                        gq.issue(e, 1, [e.dma_start(out=H["rdbg"][:, 0:512], in_=lgall[:, :]),
                                        e.dma_start(out=H["rdbg"][:, 512:1024], in_=routeA[:, :]),
                                        e.dma_start(out=H["tdbg"][:, :], in_=lnx[:, :])])
                    gq.wait_all(e)
            if upto == "route":
                return

        bes4 = ExitStack()
        bes4.__enter__()
        B4 = lambda *a: bes4.enter_context(nc.sbuf_tensor(*a))
        w1sb = [B4(f"p4_w1{i}", [128, NKT*2048], BF16) for i in range(2)]
        w2sb = [B4(f"p4_w2{i}", [128, NKT*1024], BF16) for i in range(2)]
        b1t = [B4(f"p4_b1{i}", [128, 16], F32) for i in range(2)]
        b1l = [B4(f"p4_b1l{i}", [128, 8], F32) for i in range(2)]
        rob = [B4(f"p4_ro{i}", [128, T], BF16) for i in range(2)]
        abuf = B4("p4_a", [128, 8*512], BF16)
        tg = [B4(f"p4_tg{i}", [128, 512], BF16) for i in range(2)]
        tl = [B4(f"p4_tl{i}", [128, 512], BF16) for i in range(2)]
        ts_ = [B4(f"p4_ts{i}", [128, 512], BF16) for i in range(2)]
        ps_hg = [PS(f"p4_hg{i}", [128, 512], F32) for i in range(2)]
        ps_hl = [PS(f"p4_hl{i}", [128, 512], F32) for i in range(2)]
        ps_y = [PS(f"p4_y{i}", [128, 512], F32) for i in range(2)]

        wQ, roQ = ctx.q("qSY"), ctx.q("qGA")

        def wload(e, ex):
            wQ.issue(e, ex, [
                e.dma_start(out=w1sb[ex % 2][:, :].rearrange("p (k i) -> p k i", k=NKT),
                            in_=H["w1all"][ex].rearrange("(k p) i -> p k i", p=128)),
                e.dma_start(out=w2sb[ex % 2][:, :].rearrange("p (k i) -> p k i", k=NKT),
                            in_=H["w2all"][ex].rearrange("(k p) i -> p k i", p=128)),
                e.dma_start(out=b1t[ex % 2][:, :], in_=H["b1all"][ex])])

        def roload(e, ex):
            roQ.issue(e, ex, [e.dma_start(out=rob[ex % 2][:, :],
                                          in_=H["rodr"][ex:ex+1, :].partition_broadcast(128))])

        for ex in range(NE):
            e_mm0, e_evs0, e_evv0 = ctx.mm.n, ctx.evs.n, ctx.evv.n
            with ctx.block(f"p4_e{ex}") as b:
                @b.sync
                def _(e):
                    if ex == 0:
                        wload(e, 0)
                        wload(e, 1)
                    elif ex + 1 < NE:
                        wload(e, ex + 1)

                @b.gpsimd
                def _(e):
                    if ex == 0:
                        roload(e, 0)
                        roload(e, 1)
                    elif ex + 1 < NE:
                        roload(e, ex + 1)

                @b.scalar
                def _(e):
                    e.wait_ge(wQ.sem(ex).h, wQ.sem(ex).n)
                    e.activation(out=b1l[ex % 2][:, :], in_=b1t[ex % 2][:, 8:16], func=AF.Identity,
                                 bias=ctx.c(1.0)).then_inc(ctx.evs.h, 1)
                    ctx.evs.inc()
                    for c in range(NTT):
                        for j in range(8):
                            i = c*8 + j
                            e.wait_ge(ctx.mm.h, e_mm0 + c*24 + 2*j + 2)
                            if i >= 2:
                                e.wait_ge(ctx.evv.h, e_evv0 + 2*(i - 2) + 2)
                            e.activation(out=tg[j % 2][:, :], in_=ps_hg[j % 2][:, :],
                                         func=AF.Identity, bias=b1t[ex % 2][:, j:j+1])
                            e.activation(out=tl[j % 2][:, :], in_=ps_hl[j % 2][:, :],
                                         func=AF.Identity,
                                         bias=b1l[ex % 2][:, j:j+1]).then_inc(ctx.evs.h, 1)
                            e.wait_ge(ctx.evv.h, e_evv0 + c*24 + 2*j + 1)
                            e.activation(out=ts_[j % 2][:, :], in_=tg[j % 2][:, :],
                                         func=AF.Sigmoid, scale=1.702).then_inc(ctx.evs.h, 1)
                    ctx.evs.inc(NTT*8*2)

                @b.vector
                def _(e):
                    e.wait_ge(roQ.sem(ex).h, roQ.sem(ex).n)
                    for c in range(NTT):
                        for j in range(8):
                            i = c*8 + j
                            if j == 0 and c >= 1:
                                e.wait_ge(ctx.mm.h, e_mm0 + (c - 1)*24 + 24)
                            e.wait_ge(ctx.evs.h, e_evs0 + 1 + 2*i + 1)
                            e.tensor_scalar(out=tg[j % 2][:, :], in0=tg[j % 2][:, :],
                                            scalar1=7.0, scalar2=None, op0=ALU.min)
                            e.tensor_scalar(out=tl[j % 2][:, :], in0=tl[j % 2][:, :],
                                            scalar1=-6.0, scalar2=8.0, op0=ALU.max,
                                            op1=ALU.min).then_inc(ctx.evv.h, 1)
                            e.wait_ge(ctx.evs.h, e_evs0 + 1 + 2*i + 2)
                            e.tensor_tensor(out=tg[j % 2][:, :], in0=tg[j % 2][:, :],
                                            in1=ts_[j % 2][:, :], op=ALU.mult)
                            e.tensor_tensor(out=tg[j % 2][:, :], in0=tg[j % 2][:, :],
                                            in1=tl[j % 2][:, :], op=ALU.mult)
                            e.tensor_tensor(out=abuf[:, j*512:(j+1)*512], in0=tg[j % 2][:, :],
                                            in1=rob[ex % 2][:, c*512:(c+1)*512],
                                            op=ALU.mult).then_inc(ctx.evv.h, 1)
                        for mty in range(8):
                            e.wait_ge(ctx.mm.h, e_mm0 + c*24 + 16 + mty + 1)
                            sl = slice(mty*T + c*512, mty*T + c*512 + 512)
                            e.tensor_tensor(out=acc[:, sl], in0=acc[:, sl],
                                            in1=ps_y[mty % 2][:, :], op=ALU.add).then_inc(ctx.evv.h, 1)
                    ctx.evv.inc(NTT*24)

                @b.tensor
                def _(e):
                    e.wait_ge(wQ.sem(ex).h, wQ.sem(ex).n)
                    for c in range(NTT):
                        for j in range(8):
                            i = c*8 + j
                            if i >= 2:
                                e.wait_ge(ctx.evs.h, e_evs0 + 1 + 2*(i - 2) + 1)
                            for kt in range(NKT):
                                lhs = w1sb[ex % 2][:, kt*2048 + j*128: kt*2048 + j*128 + 128]
                                e.ldweights(lhs)
                                ins = e.matmul(out=ps_hg[j % 2][:, :], lhsT=lhs,
                                               rhs=lnx[:, kt*T + c*512: kt*T + c*512 + 512],
                                               start=(kt == 0), stop=(kt == NKT-1))
                            ins.then_inc(ctx.mm.h, 1)
                            for kt in range(NKT):
                                lhs = w1sb[ex % 2][:, kt*2048 + (j+8)*128: kt*2048 + (j+8)*128 + 128]
                                e.ldweights(lhs)
                                ins = e.matmul(out=ps_hl[j % 2][:, :], lhsT=lhs,
                                               rhs=lnx[:, kt*T + c*512: kt*T + c*512 + 512],
                                               start=(kt == 0), stop=(kt == NKT-1))
                            ins.then_inc(ctx.mm.h, 1)
                        e.wait_ge(ctx.evv.h, e_evv0 + c*24 + 16)
                        for mty in range(8):
                            if mty >= 2:
                                e.wait_ge(ctx.evv.h, e_evv0 + c*24 + 16 + mty - 1)
                            elif c >= 1:
                                e.wait_ge(ctx.evv.h, e_evv0 + (c - 1)*24 + 16 + mty + 7)
                            for kt in range(NKT):
                                lhs = w2sb[ex % 2][:, kt*1024 + mty*128: kt*1024 + mty*128 + 128]
                                e.ldweights(lhs)
                                ins = e.matmul(out=ps_y[mty % 2][:, :], lhsT=lhs,
                                               rhs=abuf[:, kt*512:(kt+1)*512],
                                               start=(kt == 0), stop=(kt == NKT-1))
                            ins.then_inc(ctx.mm.h, 1)
                    ctx.mm.inc(NTT*24)

        bes4.__exit__(None, None, None)

        b2sb = SB("p4_b2", [32, 1024], BF16)
        rrt = [SB(f"p4_rr{i}", [128, 512], F32) for i in range(2)]
        zt = [SB(f"p4_z{i}", [128, 512], F32) for i in range(2)]
        with ctx.block("p4_b2ld") as b:
            @b.gpsimd
            def _(e):
                gq = ctx.q("qGA")
                gq.issue(e, 0, [e.dma_start(out=b2sb[:, :], in_=H["b2all"][:, :])])
                gq.wait_all(e)

        rrQ, ztQ = ctx.q("qGA"), ctx.q("qGB")
        brr, bzt = rrQ.snap(), ztQ.snap()
        b_mm, b_evv = ctx.mm.n, ctx.evv.n
        with ctx.block("p4_out") as b:
            @b.tensor
            def _(e):
                for c in range(NTT):
                    for mty in range(8):
                        i = c*8 + mty
                        if i >= 2:
                            e.wait_ge(ctx.evv.h, b_evv + i - 1)
                        lhs = b2sb[:, mty*128:(mty+1)*128]
                        e.ldweights(lhs)
                        ins = e.matmul(out=ps_y[i % 2][:, :], lhsT=lhs,
                                       rhs=routeTb[:, c*512:(c+1)*512],
                                       start=True, stop=True)
                        ins.then_inc(ctx.mm.h, 1)
                ctx.mm.inc(NTT*8)

            @b.gpsimd
            def _(e):
                for c in range(NTT):
                    for mty in range(8):
                        i = c*8 + mty
                        if i >= 2:
                            e.wait_ge(ctx.evv.h, b_evv + i - 1)
                        rrQ.issue(e, i, [e.dma_start(
                            out=rrt[i % 2][:, :],
                            in_=H["rdr"][mty*128:(mty+1)*128, c*512:(c+1)*512])])
                        e.wait_ge(ctx.evv.h, b_evv + i + 1)
                        ztQ.issue(e, i, [e.dma_start(
                            out=H["xout"][mty*128:(mty+1)*128, c*512:(c+1)*512],
                            in_=zt[i % 2][:, :])])
                ztQ.wait_all(e)

            @b.vector
            def _(e):
                for c in range(NTT):
                    for mty in range(8):
                        i = c*8 + mty
                        e.wait_ge(ctx.mm.h, b_mm + i + 1)
                        e.wait_ge(rrQ.sem(i).h, rrQ.ft(brr, i))
                        if i >= 2:
                            e.wait_ge(ztQ.sem(i).h, ztQ.ft(bzt, i - 2))
                        sl = slice(mty*T + c*512, mty*T + c*512 + 512)
                        e.tensor_tensor(out=zt[i % 2][:, :], in0=ps_y[i % 2][:, :],
                                        in1=acc[:, sl], op=ALU.add)
                        e.tensor_tensor(out=zt[i % 2][:, :], in0=zt[i % 2][:, :],
                                        in1=rrt[i % 2][:, :], op=ALU.add).then_inc(ctx.evv.h, 1)
                ctx.evv.inc(NTT*8)
    return



def _new_nc():
    return bass.Bass("TRN2", target_bir_lowering=False, detect_race_conditions=False)


def build_attn_program(l1, half, upto="all"):
    nc = _new_nc()
    H = {"half": half}
    H["xow"] = nc.dram_tensor("xow", [D, T], F32, kind="ExternalInput").ap()
    H["xot"] = nc.dram_tensor("xot", [D, T], F32, kind="ExternalInput").ap()
    H["wq"] = nc.dram_tensor("wq", [D, D], F32, kind="ExternalInput").ap()
    H["wk"] = nc.dram_tensor("wk", [D, D], F32, kind="ExternalInput").ap()
    H["wv"] = nc.dram_tensor("wv", [D, D], F32, kind="ExternalInput").ap()
    H["wo"] = nc.dram_tensor("wo", [D, D], F32, kind="ExternalInput").ap()
    if not l1:
        H["cmask"] = nc.dram_tensor("cmask", [4, 4, 128, 512], BF16, kind="ExternalInput").ap()
        H["biasM"] = nc.dram_tensor("biasM", [128, 128], F32, kind="ExternalInput").ap()
        H["rdr"] = nc.dram_tensor("rdr", [D, T], F32).ap()
    else:
        H["iotaK"] = nc.dram_tensor("iotaK", [128, 32], F32, kind="ExternalInput").ap()
        H["slopes"] = nc.dram_tensor("slopes", [1, 16], F32, kind="ExternalInput").ap()
        H["bq"] = nc.dram_tensor("bq", [128, 8], F32, kind="ExternalInput").ap()
        H["bk"] = nc.dram_tensor("bk", [128, 8], F32, kind="ExternalInput").ap()
        H["bv"] = nc.dram_tensor("bv", [1, D], F32, kind="ExternalInput").ap()
        H["bo"] = nc.dram_tensor("bo", [128, 8], F32, kind="ExternalInput").ap()
    H["xout"] = nc.dram_tensor("xout", [D, T], F32, kind="ExternalOutput").ap()
    if upto == "ctx":
        H["ctxdbg"] = nc.dram_tensor("ctxdbg", [128, 8*T], BF16, kind="ExternalOutput").ap()
        H["pdbg"] = nc.dram_tensor("pdbg", [128, 1024], BF16, kind="ExternalOutput").ap()
        H["kdbg"] = nc.dram_tensor("kdbg", [128, S], BF16, kind="ExternalOutput").ap()
        H["qdbg"] = nc.dram_tensor("qdbg", [128, T], BF16, kind="ExternalOutput").ap()
        H["vdbg"] = nc.dram_tensor("vdbg", [128, 32*130], BF16, kind="ExternalOutput").ap()
    if upto == "ln":
        H["lndbg"] = nc.dram_tensor("lndbg", [128, NKT*S], BF16, kind="ExternalOutput").ap()
    es = ExitStack()
    with es:
        ctx = Ctx(nc, es, consts=CONSTS)
        emit_attn(ctx, H, l1, upto=upto)
    return nc


def prep_p1_inputs(inputs, core):
    b, h = core // 2, core % 2
    out = {}
    out["wq"] = np.ascontiguousarray(np.asarray(inputs["L0_attn_Wq"])[PI, :]).astype(np.float32)
    out["wk"] = np.ascontiguousarray(np.asarray(inputs["L0_attn_Wk"])[PI, :]).astype(np.float32)
    out["wv"] = np.ascontiguousarray(np.asarray(inputs["L0_attn_Wv"])[PI, :]).astype(np.float32)
    out["wo"] = np.ascontiguousarray(np.asarray(inputs["L0_attn_Wo"])[:, PI]).astype(np.float32)
    cm = np.zeros((4, 4, 128, 512), dtype=np.float32)
    bM = np.zeros((128, 128), dtype=np.float32)
    for qs in range(4):
        for j in range(4):
            kb = 4*qs + j
            key = 128*kb + np.arange(128)[:, None]
            ql = 512*qs + np.arange(512)[None, :]
            cm[qs, j] = (key <= ql)
        for kb in range(32):
            if kb < 16:
                if 128*kb > 512*qs + 511:
                    bM[:, qs*32 + kb] = -30000.0
            else:
                if h == 0:
                    bM[:, qs*32 + kb] = -30000.0
    out["cmask"] = cm.astype(BF)
    out["biasM"] = bM
    return out


def prep_p3_inputs(inputs, core):
    out = {}
    out["wq"] = np.ascontiguousarray(np.asarray(inputs["L1_attn_Wq"])[:, PI].T).astype(np.float32)
    out["wk"] = np.ascontiguousarray(np.asarray(inputs["L1_attn_Wk"])[:, PI].T).astype(np.float32)
    out["wv"] = np.ascontiguousarray(np.asarray(inputs["L1_attn_Wv"])[:, PI].T).astype(np.float32)
    out["wo"] = np.ascontiguousarray(np.asarray(inputs["L1_attn_Wo"])[PI, :].T).astype(np.float32)
    out["bq"] = pcols(np.asarray(inputs["L1_attn_bq"]).astype(np.float32))
    out["bk"] = pcols(np.asarray(inputs["L1_attn_bk"]).astype(np.float32))
    out["bv"] = np.ascontiguousarray(np.asarray(inputs["L1_attn_bv"]).astype(np.float32))[None, :]
    out["bo"] = pcols(np.asarray(inputs["L1_attn_bo"])[PI].astype(np.float32))
    out["slopes"] = np.ascontiguousarray(np.asarray(inputs["L1_attn_slopes"]).astype(np.float32))[None, :]
    h = core % 2
    j = 128*(np.arange(32)[None, :]) + np.arange(128)[:, None]
    keyglobal = np.where(j < T, 2048*h + j, 2048*(1 - h) + (j - T))
    out["iotaK"] = np.ascontiguousarray((keyglobal - (S - 1)).astype(np.float32))
    return out


LN2 = float(np.log(2.0))
CONSTS = [-128.0, EPS, -127.0*LN2, 1.0]


def build_p0_program():
    nc = _new_nc()
    H = {}
    H["w1b"] = nc.dram_tensor("w1b", [4, 512, 2048], U8, kind="ExternalInput").ap()
    H["s1u"] = nc.dram_tensor("s1u", [4, 32, 2048], U8, kind="ExternalInput").ap()
    H["w2b"] = nc.dram_tensor("w2b", [4, 512, 1024], U8, kind="ExternalInput").ap()
    H["s2u"] = nc.dram_tensor("s2u", [4, 32, 1024], U8, kind="ExternalInput").ap()
    H["w1dec"] = nc.dram_tensor("w1dec", [4, 1024, 2048], BF16, kind="ExternalOutput").ap()
    H["w2dec"] = nc.dram_tensor("w2dec", [4, 1024, 1024], BF16, kind="ExternalOutput").ap()
    es = ExitStack()
    with es:
        ctx = Ctx(nc, es, consts=CONSTS)
        emit_p0(ctx, H)
    return nc


def prep_p0_inputs(inputs, core):
    ex0 = 4*core
    w1b = np.zeros((4, 512, 2048), np.uint8)
    s1u = np.zeros((4, 32, 2048), np.uint8)
    w2b = np.zeros((4, 512, 1024), np.uint8)
    s2u = np.zeros((4, 32, 1024), np.uint8)
    b1 = np.asarray(inputs["L1_ffn_mlp1_blocks"])
    sc1 = np.asarray(inputs["L1_ffn_mlp1_scales"])
    b2 = np.asarray(inputs["L1_ffn_mlp2_blocks"])
    sc2 = np.asarray(inputs["L1_ffn_mlp2_scales"])
    for j in range(4):
        e = ex0 + j
        w1b[j] = b1[e][PI2].reshape(2048, 512).T.astype(np.uint8)
        s1u[j] = sc1[e][PI2].T.astype(np.uint8)
        w2b[j] = b2[e][PI].reshape(1024, 512).T.astype(np.uint8)
        s2u[j] = sc2[e][PI].T.astype(np.uint8)
    return dict(w1b=w1b, s1u=s1u, w2b=w2b, s2u=s2u)


def build_p4_program(upto="all"):
    nc = _new_nc()
    H = {}
    H["xin"] = nc.dram_tensor("xin", [D, T], F32, kind="ExternalInput").ap()
    H["w1all"] = nc.dram_tensor("w1all", [32, 1024, 2048], BF16, kind="ExternalInput").ap()
    H["w2all"] = nc.dram_tensor("w2all", [32, 1024, 1024], BF16, kind="ExternalInput").ap()
    H["b1all"] = nc.dram_tensor("b1all", [32, 128, 16], F32, kind="ExternalInput").ap()
    H["b2all"] = nc.dram_tensor("b2all", [32, 1024], F32, kind="ExternalInput").ap()
    H["nw"] = nc.dram_tensor("nw", [128, NKT], F32, kind="ExternalInput").ap()
    H["gwT"] = nc.dram_tensor("gwT", [D, 32], F32, kind="ExternalInput").ap()
    H["gateb"] = nc.dram_tensor("gateb", [1, 32], F32, kind="ExternalInput").ap()
    H["rdr"] = nc.dram_tensor("rdr", [D, T], F32).ap()
    H["rodr"] = nc.dram_tensor("rodr", [32, T], F32).ap()
    H["xout"] = nc.dram_tensor("xout", [D, T], F32, kind="ExternalOutput").ap()
    if upto == "route":
        H["rdbg"] = nc.dram_tensor("rdbg", [128, 1024], F32, kind="ExternalOutput").ap()
        H["tdbg"] = nc.dram_tensor("tdbg", [128, NKT*T], BF16, kind="ExternalOutput").ap()
    es = ExitStack()
    with es:
        ctx = Ctx(nc, es, consts=CONSTS)
        emit_p4(ctx, H, upto=upto)
    return nc


def prep_p4_shared(inputs):
    out = {}
    b1 = np.asarray(inputs["L1_ffn_mlp1_bias"])[:, PI2]
    out["b1all"] = np.ascontiguousarray(
        b1.reshape(32, 16, 128).transpose(0, 2, 1)).astype(np.float32)
    out["b2all"] = np.ascontiguousarray(np.asarray(inputs["L1_ffn_mlp2_bias"])[:, PI]).astype(np.float32)
    out["nw"] = pcols(np.asarray(inputs["L1_ffn_norm_w"])[PI].astype(np.float32))
    out["gwT"] = np.ascontiguousarray(np.asarray(inputs["L1_ffn_gate_w"])[:, PI].T).astype(np.float32)
    out["gateb"] = np.ascontiguousarray(np.asarray(inputs["L1_ffn_gate_b"]).astype(np.float32))[None, :]
    return out


def build_p2_program(upto="all"):
    nc = _new_nc()
    H = {}
    H["xin"] = nc.dram_tensor("xin", [D, T], F32, kind="ExternalInput").ap()
    H["fc1q"] = nc.dram_tensor("fc1q", [D, 4096], U8, kind="ExternalInput").ap()
    H["s1rows"] = nc.dram_tensor("s1rows", [16, 4096], F32, kind="ExternalInput").ap()
    H["fc1b"] = nc.dram_tensor("fc1b", [128, 32], F32, kind="ExternalInput").ap()
    H["fc2q"] = nc.dram_tensor("fc2q", [4096, D], U8, kind="ExternalInput").ap()
    H["fc2s"] = nc.dram_tensor("fc2s", [32, D], F32, kind="ExternalInput").ap()
    H["fc2b"] = nc.dram_tensor("fc2b", [128, NKT], F32, kind="ExternalInput").ap()
    H["gamma"] = nc.dram_tensor("gamma", [128, NKT], F32, kind="ExternalInput").ap()
    H["beta"] = nc.dram_tensor("beta", [128, NKT], F32, kind="ExternalInput").ap()
    H["w1dec"] = nc.dram_tensor("w1dec", [8, 128, 4096], BF16).ap()
    H["w2dec"] = nc.dram_tensor("w2dec", [32, 128, 1024], BF16).ap()
    H["rdr"] = nc.dram_tensor("rdr", [D, T], F32).ap()
    H["xout"] = nc.dram_tensor("xout", [D, T], F32, kind="ExternalOutput").ap()
    if upto == "fc1":
        H["ydbg"] = nc.dram_tensor("ydbg", [128, 32*512], BF16, kind="ExternalOutput").ap()
    if upto == "ln":
        H["hdbg"] = nc.dram_tensor("hdbg", [D, T], BF16, kind="ExternalOutput").ap()
        H["rdr"] = nc.dram_tensor("rdr2", [D, T], F32, kind="ExternalOutput").ap()
        H["bcdbg"] = nc.dram_tensor("bcdbg", [128, 4*T], F32, kind="ExternalOutput").ap()
        H["lnbdbg"] = nc.dram_tensor("lnbdbg", [128, 2*T], F32, kind="ExternalOutput").ap()
    es = ExitStack()
    with es:
        ctx = Ctx(nc, es, consts=CONSTS)
        emit_p2(ctx, H, upto=upto)
    return nc


def prep_p2_inputs(inputs):
    fc1q = np.ascontiguousarray(inputs["L0_ffn_fc1_q"][PI, :]).astype(np.uint8)
    s1 = inputs["L0_ffn_fc1_scales"].astype(np.float32)
    order = []
    for g in range(8):
        gg = g if g < 4 else g - 4
        order += [2*gg, 2*gg + 1]
    s1rows = np.ascontiguousarray(s1[order])
    fc2q = np.ascontiguousarray(inputs["L0_ffn_fc2_q"][:, PI]).astype(np.uint8)
    fc2s = np.ascontiguousarray(inputs["L0_ffn_fc2_scales"][:, PI]).astype(np.float32)
    return dict(
        fc1q=fc1q, s1rows=s1rows,
        fc1b=pcols(inputs["L0_ffn_fc1_bias"].astype(np.float32)),
        fc2q=fc2q, fc2s=fc2s,
        fc2b=pcols(np.asarray(inputs["L0_ffn_fc2_bias"])[PI].astype(np.float32)),
        gamma=pcols(np.asarray(inputs["L0_ffn_ln_gamma"])[PI].astype(np.float32)),
        beta=pcols(np.asarray(inputs["L0_ffn_ln_beta"])[PI].astype(np.float32)),
    )


def pcols(v):
    return np.ascontiguousarray(v.reshape(-1, 128).T)


def to_fm(x_bt):
    return np.ascontiguousarray(x_bt[:, PI].T).astype(np.float32)


def from_fm(x_fm):
    return np.ascontiguousarray(x_fm.T[:, PINV])


def kernel(**inputs):
    from concourse.bass_utils import run_bass_kernel_spmd
    cores = list(range(8))
    x0 = np.asarray(inputs["L0_attn_x"], np.float32)
    B, S, _ = x0.shape

    nc0 = build_p0_program()
    maps0 = [prep_p0_inputs(inputs, c) for c in cores]
    r0 = run_bass_kernel_spmd(nc0, maps0, core_ids=cores)
    w1all = np.concatenate([np.asarray(r0.results[c]["w1dec"]).reshape(4, 1024, 2048)
                            for c in cores])
    w2all = np.concatenate([np.asarray(r0.results[c]["w2dec"]).reshape(4, 1024, 1024)
                            for c in cores])

    nc1 = build_attn_program(l1=False, half=0)
    maps1 = []
    for c in cores:
        b, h = c // 2, c % 2
        m = prep_p1_inputs(inputs, c)
        m["xow"] = to_fm(x0[b, 2048*h:2048*(h+1), :])
        m["xot"] = to_fm(x0[b, 2048*(1-h):2048*(2-h), :])
        maps1.append(m)
    r1 = run_bass_kernel_spmd(nc1, maps1, core_ids=cores)
    x1s = [np.asarray(r1.results[c]["xout"]) for c in cores]

    nc2 = build_p2_program()
    sh2 = prep_p2_inputs(inputs)
    maps2 = [dict(sh2, xin=x1s[c]) for c in cores]
    r2 = run_bass_kernel_spmd(nc2, maps2, core_ids=cores)
    x2s = [np.asarray(r2.results[c]["xout"]) for c in cores]

    nc3 = build_attn_program(l1=True, half=0)
    maps3 = []
    for c in cores:
        m = prep_p3_inputs(inputs, c)
        m["xow"] = x2s[c]
        m["xot"] = x2s[c ^ 1]
        maps3.append(m)
    r3 = run_bass_kernel_spmd(nc3, maps3, core_ids=cores)
    x3s = [np.asarray(r3.results[c]["xout"]) for c in cores]

    nc4 = build_p4_program()
    sh4 = prep_p4_shared(inputs)
    sh4["w1all"] = w1all
    sh4["w2all"] = w2all
    maps4 = [dict(sh4, xin=x3s[c]) for c in cores]
    r4 = run_bass_kernel_spmd(nc4, maps4, core_ids=cores)

    out = np.empty((B, S, 1024), np.float32)
    for c in cores:
        b, h = c // 2, c % 2
        out[b, 2048*h:2048*(h+1), :] = from_fm(np.asarray(r4.results[c]["xout"]))
    return out
